# version 01
# baseline (speedup 1.0000x reference)
import numpy as np
import ml_dtypes
import concourse.bass as bass
import concourse.mybir as mybir
from concourse.bass_utils import run_bass_kernel_spmd

F32 = mybir.dt.float32
BF16 = mybir.dt.bfloat16
AF = mybir.ActivationFunctionType
ALU = mybir.AluOpType
AX = mybir.AxisListType

D = 2048
L = 2048
NSEQ = 2
NH = 32
HF = 64
CS = 16
G = 8
SC = CS * G
NSC = L // SC
NPAIR = NH // 2
NCHUNK = D // 128
NBLK = L // 128
NS = L // 512
MAGIC = float(np.float32(1.5 * 2 ** 23))
LN_EPS = 1e-5

DEBUG_DUMPS = False
BUILD_PHASES = ("A", "B", "C", "D")


class Ctr:

    def __init__(self, nc, name):
        self.h = nc.alloc_semaphore(name=name)
        self.v = 0

    def inc(self, instr, n=1):
        instr.then_inc(self.h, n)
        self.v += n
        return self.v


def build_program():
    nc = bass.Bass("TRN2", target_bir_lowering=False)

    io = {}
    io["xT"] = nc.dram_tensor("xT", [NSEQ, D, L], F32, kind="ExternalInput").ap()
    io["x"] = nc.dram_tensor("x", [NSEQ, L, D], F32, kind="ExternalInput").ap()
    for nm, shp in (("wqT", [D, D]), ("wkT", [D, D]), ("wvT", [D, D]),
                    ("woT", [D, D]), ("wilrT", [D, NH]), ("bilr_bc", [128, NH]),
                    ("w1bd", [NPAIR, 128, 128]), ("wf", [D, D]), ("wfT", [D, D]),
                    ("fbias_row", [1, D]), ("maskoff_neg", [128, 128]),
                    ("maskdiag_neg", [128, 128]), ("masksv", [128, 128]), ("tokidx", [128, 1]),
                    ("epsc", [128, 1])):
        io[nm] = nc.dram_tensor(nm, shp, F32, kind="ExternalInput").ap()
    io["onesbf"] = nc.dram_tensor("onesbf", [1, 128], BF16, kind="ExternalInput").ap()
    io["selbc"] = nc.dram_tensor("selbc", [32, 2 * D], F32, kind="ExternalInput").ap()
    io["onesf32"] = nc.dram_tensor("onesf32", [1, 128], F32, kind="ExternalInput").ap()
    io["identf32"] = nc.dram_tensor("identf32", [128, 128], F32, kind="ExternalInput").ap()
    io["identbf"] = nc.dram_tensor("identbf", [128, 128], BF16, kind="ExternalInput").ap()

    io["out"] = nc.dram_tensor("out", [NSEQ, L, D], F32, kind="ExternalOutput").ap()

    skind = "ExternalOutput" if DEBUG_DUMPS else "Internal"
    io["sXC"] = nc.dram_tensor("sXC", [NSEQ, D, L], BF16, kind=skind).ap()
    io["sXB"] = nc.dram_tensor("sXB", [NSEQ, D, L], BF16, kind=skind).ap()
    io["sXA"] = nc.dram_tensor("sXA", [NSEQ, L, D], BF16, kind=skind).ap()
    io["sZ"] = nc.dram_tensor("sZ", [NSEQ, L, D], BF16, kind=skind).ap()
    io["sH"] = nc.dram_tensor("sH", [NSEQ, L, D], BF16, kind=skind).ap()
    io["sHq"] = nc.dram_tensor("sHq", [NSEQ, L, D], BF16, kind=skind).ap()
    if DEBUG_DUMPS:
        io["dCO"] = nc.dram_tensor("dCO", [128, NSEQ * NBLK * NH], F32,
                                   kind="ExternalOutput").ap()
        io["dTOK"] = nc.dram_tensor("dTOK", [128, 2], F32, kind="ExternalOutput").ap()
        io["dWS"] = nc.dram_tensor("dWS", [128, 2 * D], F32, kind="ExternalOutput").ap()
        io["dBI"] = nc.dram_tensor("dBI", [128, D], F32, kind="ExternalOutput").ap()
        io["dWQ"] = nc.dram_tensor("dWQ", [128, NCHUNK * D], BF16, kind="ExternalOutput").ap()

    ps = {}
    ps["coeff"] = nc.alloc_sbuf_tensor("coeff_sb", [128, NSEQ * NBLK * NH], F32)
    ps["mko"] = nc.alloc_sbuf_tensor("mko_sb", [128, 128], F32)
    ps["mkd"] = nc.alloc_sbuf_tensor("mkd_sb", [128, 128], F32)
    ps["msv"] = nc.alloc_sbuf_tensor("msv_sb", [128, 128], F32)
    ps["tokidx"] = nc.alloc_sbuf_tensor("tokidx_sb", [128, 1], F32)
    ps["tokc"] = nc.alloc_sbuf_tensor("tokc_sb", [128, 1], F32)
    ps["eps"] = nc.alloc_sbuf_tensor("eps_sb", [128, 1], F32)
    ps["ones"] = nc.alloc_sbuf_tensor("ones_sb", [1, 128], BF16)
    ps["ident"] = nc.alloc_sbuf_tensor("ident_sb", [128, 128], BF16)
    ps["bilr"] = nc.alloc_sbuf_tensor("bilr_sb", [128, NH], F32)
    ps["w1f"] = nc.alloc_sbuf_tensor("w1f", [128, NPAIR * 128], F32)
    ps["w1b"] = nc.alloc_sbuf_tensor("w1b", [128, NPAIR * 128], BF16)

    pre = Ctr(nc, "preload")
    with nc.Block() as blk:
        @blk.sync
        def _(sy):
            for dst, src in ((ps["mko"], io["maskoff_neg"]), (ps["mkd"], io["maskdiag_neg"]),
                             (ps["msv"], io["masksv"]),
                             (ps["tokidx"], io["tokidx"]), (ps["eps"], io["epsc"]),
                             (ps["ones"], io["onesbf"]), (ps["ident"], io["identbf"]),
                             (ps["bilr"], io["bilr_bc"])):
                pre.inc(sy.dma_start(out=dst[:, :], in_=src[:, :]), 16)
            sy.wait_ge(pre.h, pre.v)

        prev = Ctr(nc, "preloadv")

        @blk.vector
        def _(v):
            v.wait_ge(pre.h, pre.v)
            prev.inc(v.reciprocal(ps["tokc"][:, 0:1], ps["tokidx"][:, 0:1]), 1)
            v.wait_ge(prev.h, prev.v)
            v.tensor_scalar_mul(ps["tokc"][:, 0:1], ps["tokc"][:, 0:1], 1.0 / HF)

    if "A" in BUILD_PHASES:
        for seq in range(NSEQ):
            _emit_A(nc, seq, io, ps)
    if "B" in BUILD_PHASES:
        for seq in range(NSEQ):
            _emit_B(nc, seq, io, ps)
    if "C" in BUILD_PHASES:
        for seq in range(NSEQ):
            _emit_C(nc, seq, io, ps)
    if "D" in BUILD_PHASES or "Di" in BUILD_PHASES:
        from contextlib import ExitStack
        _des = ExitStack()
        ps["wq"] = _des.enter_context(nc.sbuf_tensor("wq_sb", [128, NCHUNK * D], BF16))
        ps["wsb"] = _des.enter_context(nc.sbuf_tensor("wsb_sb", [128, 2 * D], F32))
        ps["biasb"] = _des.enter_context(nc.sbuf_tensor("biasb_sb", [128, D], F32))
        _emit_Dinit(nc, io, ps)
        if "D" in BUILD_PHASES or "Dq" in BUILD_PHASES:
            for seq in range(NSEQ):
                _emit_D(nc, seq, io, ps, mm_part=("D" in BUILD_PHASES))
        _des.close()

    if DEBUG_DUMPS and "A" in BUILD_PHASES:
        dsem = Ctr(nc, "dump")
        with nc.Block() as blk:
            @blk.gpsimd
            def _(g):
                dsem.inc(g.dma_start(out=io["dCO"][:, :], in_=ps["coeff"][:, :]), 16)
                if "D" in BUILD_PHASES:
                    dsem.inc(g.dma_start(out=io["dWS"][:, :], in_=ps["wsb"][:, :]), 16)
                    dsem.inc(g.dma_start(out=io["dBI"][:, :], in_=ps["biasb"][:, :]), 16)
                    dsem.inc(g.dma_start(out=io["dWQ"][:, :], in_=ps["wq"][:, :]), 16)
                dsem.inc(g.dma_start(out=io["dTOK"][:, 0:1], in_=ps["tokidx"][:, 0:1], allow_slow_non_contiguous=True), 16)
                dsem.inc(g.dma_start(out=io["dTOK"][:, 1:2], in_=ps["tokc"][:, 0:1], allow_slow_non_contiguous=True), 16)
                g.wait_ge(dsem.h, dsem.v)
        if "_des" in ps:
            ps["_des"].close()

    if "D" not in BUILD_PHASES:
        z = Ctr(nc, "zout")
        with nc.Block() as blk:
            @blk.vector
            def _(v):
                pass
            @blk.gpsimd
            def _(g):
                pass

    return nc


def _emit_A(nc, seq, io, ps):
    xT, sXC, sXB, sXA = io["xT"], io["sXC"], io["sXB"], io["sXA"]
    coeff_sb, bilr_sb, tokidx_sb = ps["coeff"], ps["bilr"], ps["tokc"]

    NITER_A1 = 2 * NCHUNK
    NITER = NITER_A1 + NBLK

    with (
        nc.sbuf_tensor(f"xt_sb{seq}", [128, NCHUNK * L], BF16) as xt_sb,
        nc.sbuf_tensor(f"wsl_sb{seq}", [128, 2 * NCHUNK * 128], BF16) as wsl_sb,
        nc.sbuf_tensor(f"stage_sb{seq}", [128, 2 * L], BF16) as stage_sb,
        nc.sbuf_tensor(f"wv_sb{seq}", [128, NCHUNK * D], BF16) as wv_sb,
        nc.sbuf_tensor(f"wilr_sb{seq}", [128, NCHUNK * NH], BF16) as wilr_sb,
        nc.sbuf_tensor(f"ilr_sb{seq}", [128, NH], F32) as ilr_sb,
        nc.psum_tensor(f"pa0{seq}", [128, 512], F32) as pa0,
        nc.psum_tensor(f"pa1{seq}", [128, 512], F32) as pa1,
        nc.psum_tensor(f"pa2{seq}", [128, 512], F32) as pa2,
        nc.psum_tensor(f"pa3{seq}", [128, 512], F32) as pa3,
        nc.psum_tensor(f"pilr{seq}", [128, NH], F32) as pilr,
        nc.Block() as blk,
    ):
        banks = [pa0, pa1, pa2, pa3]
        s_xld = Ctr(nc, f"a_xld{seq}")
        s_wld = Ctr(nc, f"a_wld{seq}")
        s_mm = Ctr(nc, f"a_mm{seq}")
        s_ev = Ctr(nc, f"a_ev{seq}")
        s_sig = Ctr(nc, f"a_sig{seq}")
        s_st = Ctr(nc, f"a_st{seq}")

        mm_bank_done = {}
        mm_ilr_done = {}
        mm_iter_done = {}
        ev_bank_done = {}
        ev_iter_done = {}
        ev_bias_done = {}
        st_done = {}

        v = 0
        for it in range(NITER):
            for ns in range(NS):
                v += 1
                mm_bank_done[(it, ns)] = v
            if it >= NITER_A1:
                v += 1
                mm_ilr_done[it - NITER_A1] = v
            mm_iter_done[it] = v
        v = 0
        for it in range(NITER):
            for ns in range(NS):
                v += 1
                ev_bank_done[(it, ns)] = v
            if it >= NITER_A1:
                v += 1
                ev_bias_done[it - NITER_A1] = v
            ev_iter_done[it] = v
        v = 0
        for it in range(NITER):
            v += 16
            st_done[it] = v

        @blk.gpsimd
        def _(g):
            for c in range(NCHUNK):
                s_xld.inc(g.dma_start(out=xt_sb[:, c * L:(c + 1) * L],
                                      in_=xT[seq, 128 * c:128 * (c + 1), :]), 16)
            for wi, wT in enumerate((io["wqT"], io["wkT"])):
                for d in range(NCHUNK):
                    it = wi * NCHUNK + d
                    buf = it % 2
                    if it >= 2:
                        g.wait_ge(s_mm.h, mm_iter_done[it - 2])
                    s_wld.inc(g.dma_start(
                        out=wsl_sb[:, buf * NCHUNK * 128:(buf + 1) * NCHUNK * 128]
                            .rearrange("p (g c) -> p g c", g=NCHUNK),
                        in_=wT[:, 128 * d:128 * (d + 1)]
                            .rearrange("(g p) c -> p g c", p=128)), 16)
            for c in range(NCHUNK):
                s_wld.inc(g.dma_start(out=wv_sb[:, c * D:(c + 1) * D],
                                      in_=io["wvT"][128 * c:128 * (c + 1), :]), 16)
            s_wld.inc(g.dma_start(
                out=wilr_sb[:, :].rearrange("p (g c) -> p g c", g=NCHUNK),
                in_=io["wilrT"][:, :].rearrange("(g p) c -> p g c", p=128)), 16)

        @blk.tensor
        def _(t):
            t.wait_ge(s_xld.h, 16 * NCHUNK)
            for wi in range(2):
                for d in range(NCHUNK):
                    it = wi * NCHUNK + d
                    buf = it % 2
                    t.wait_ge(s_wld.h, 16 * (it + 1))
                    if it >= 1:
                        t.wait_ge(s_ev.h, ev_iter_done[it - 1])
                    for ns in range(NS):
                        last = None
                        for c in range(NCHUNK):
                            last = t.matmul(
                                banks[ns][:, :],
                                wsl_sb[:, (buf * NCHUNK + c) * 128:(buf * NCHUNK + c + 1) * 128],
                                xt_sb[:, c * L + ns * 512: c * L + (ns + 1) * 512],
                                start=(c == 0), stop=(c == NCHUNK - 1))
                        last.then_inc(s_mm.h, 1)
            t.wait_ge(s_wld.h, 16 * (2 * NCHUNK + NCHUNK + 1))
            for tb in range(NBLK):
                it = NITER_A1 + tb
                t.wait_ge(s_ev.h, ev_iter_done[it - 1])
                for ns in range(NS):
                    last = None
                    for c in range(NCHUNK):
                        last = t.matmul(
                            banks[ns][:, :],
                            xt_sb[:, c * L + tb * 128: c * L + (tb + 1) * 128],
                            wv_sb[:, c * D + ns * 512: c * D + (ns + 1) * 512],
                            start=(c == 0), stop=(c == NCHUNK - 1))
                    last.then_inc(s_mm.h, 1)
                last = None
                for c in range(NCHUNK):
                    last = t.matmul(
                        pilr[:, :],
                        xt_sb[:, c * L + tb * 128: c * L + (tb + 1) * 128],
                        wilr_sb[:, c * NH:(c + 1) * NH],
                        start=(c == 0), stop=(c == NCHUNK - 1))
                last.then_inc(s_mm.h, 1)

        @blk.vector
        def _(v):
            for it in range(NITER):
                sbuf = it % 2
                if it >= 2:
                    v.wait_ge(s_st.h, st_done[it - 2])
                for ns in range(NS):
                    v.wait_ge(s_mm.h, mm_bank_done[(it, ns)])
                    s_ev.inc(v.tensor_copy(
                        stage_sb[:, sbuf * L + ns * 512: sbuf * L + (ns + 1) * 512],
                        banks[ns][:, :]), 1)
                if it >= NITER_A1:
                    tb = it - NITER_A1
                    v.wait_ge(s_mm.h, mm_ilr_done[tb])
                    if tb >= 1:
                        v.wait_ge(s_sig.h, tb)
                    s_ev.inc(v.tensor_add(ilr_sb[:, :], pilr[:, :], bilr_sb[:, :]), 1)

        @blk.scalar
        def _(sc):
            for tb in range(NBLK):
                sc.wait_ge(s_ev.h, ev_bias_done[tb])
                col = (seq * NBLK + tb) * NH
                s_sig.inc(sc.activation(coeff_sb[:, col:col + NH], ilr_sb[:, :],
                                        AF.Sigmoid), 1)

        @blk.vector
        def _(v):
            for tb in range(NBLK):
                col = (seq * NBLK + tb) * NH
                v.wait_ge(s_sig.h, tb + 1)
                s_ev.inc(v.tensor_scalar_mul(coeff_sb[:, col:col + NH],
                                             coeff_sb[:, col:col + NH],
                                             tokidx_sb[:, 0:1]), 1)

        @blk.sync
        def _(sy):
            for it in range(NITER):
                sbuf = it % 2
                sy.wait_ge(s_ev.h, ev_iter_done[it])
                if it < NCHUNK:
                    dst = sXC[seq, 128 * it:128 * (it + 1), :]
                elif it < NITER_A1:
                    dst = sXB[seq, 128 * (it - NCHUNK):128 * (it - NCHUNK + 1), :]
                else:
                    tb = it - NITER_A1
                    dst = sXA[seq, 128 * tb:128 * (tb + 1), :]
                s_st.inc(sy.dma_start(out=dst, in_=stage_sb[:, sbuf * L:(sbuf + 1) * L]), 16)
            sy.wait_ge(s_st.h, s_st.v)


def _emit_B(nc, seq, io, ps):
    sXB, sXC, sXA, sZ = io["sXB"], io["sXC"], io["sXA"], io["sZ"]
    coeff_sb, mko, mkd, msv = ps["coeff"], ps["mko"], ps["mkd"], ps["msv"]
    w1f, w1b = ps["w1f"], ps["w1b"]
    CW = NCHUNK * 128

    from contextlib import ExitStack
    _es = ExitStack()
    xbf = _es.enter_context(nc.sbuf_tensor(f"b_xbf{seq}", [128, 2 * CW], BF16))
    xcf = _es.enter_context(nc.sbuf_tensor(f"b_xcf{seq}", [128, 2 * CW], BF16))
    xbt = _es.enter_context(nc.sbuf_tensor(f"b_xbt{seq}", [128, 2 * D], BF16))
    xa = _es.enter_context(nc.sbuf_tensor(f"b_xa{seq}", [128, 2 * D], BF16))
    zout = _es.enter_context(nc.sbuf_tensor(f"b_zout{seq}", [128, 2 * D], BF16))
    stA = _es.enter_context(nc.sbuf_tensor(f"b_stA{seq}", [128, 128], BF16))
    stB = _es.enter_context(nc.sbuf_tensor(f"b_stB{seq}", [128, 128], BF16))
    qoffA = _es.enter_context(nc.sbuf_tensor(f"b_qoffA{seq}", [128, 128], BF16))
    qoffB = _es.enter_context(nc.sbuf_tensor(f"b_qoffB{seq}", [128, 128], BF16))
    gdnA = _es.enter_context(nc.sbuf_tensor(f"b_gdnA{seq}", [128, 128], BF16))
    gdnB = _es.enter_context(nc.sbuf_tensor(f"b_gdnB{seq}", [128, 128], BF16))
    z1 = _es.enter_context(nc.sbuf_tensor(f"b_z1{seq}", [128, 128], BF16))
    z1co = _es.enter_context(nc.sbuf_tensor(f"b_z1co{seq}", [128, 128], BF16))
    tmp = _es.enter_context(nc.sbuf_tensor(f"b_tmp{seq}", [128, 128], BF16))
    corr = _es.enter_context(nc.sbuf_tensor(f"b_corr{seq}", [128, 128], BF16))
    b_sA = _es.enter_context(nc.psum_tensor(f"b_sA{seq}", [128, 128], F32))
    b_sB = _es.enter_context(nc.psum_tensor(f"b_sB{seq}", [128, 128], F32))
    b_gA = _es.enter_context(nc.psum_tensor(f"b_gA{seq}", [128, 128], F32))
    b_gB = _es.enter_context(nc.psum_tensor(f"b_gB{seq}", [128, 128], F32))
    b_v = _es.enter_context(nc.psum_tensor(f"b_v{seq}", [128, 128], F32))
    b_zb = _es.enter_context(nc.psum_tensor(f"b_zb{seq}", [128, 128], F32))
    b_d = _es.enter_context(nc.psum_tensor(f"b_d{seq}", [128, 128], F32))
    b_w = _es.enter_context(nc.psum_tensor(f"b_w{seq}", [128, 128], F32))
    blk = _es.enter_context(nc.Block())
    if True:
        s_ldb = Ctr(nc, f"b_ld{seq}")
        s_w1 = Ctr(nc, f"b_w1{seq}")
        s_mm = Ctr(nc, f"b_mm{seq}")
        s_dve = Ctr(nc, f"b_dve{seq}")
        s_act = Ctr(nc, f"b_act{seq}")
        s_stz = Ctr(nc, f"b_st{seq}")

        mm = {}; dve = {}; act = {}
        vmm = vdve = vact = 0
        mm_after_sc = {}; dve_after_sc = {}
        dve_w1copy = 1
        vdve = 1
        for sc in range(NSC):
            for pr in range(NPAIR):
                vmm += 1; mm[("sg", sc, pr)] = vmm
                vmm += 1; mm[("r", sc, pr)] = vmm
                for j in range(G // 2):
                    vmm += 1; mm[("sva", sc, pr, j)] = vmm
                    vmm += 1; mm[("svc", sc, pr, j)] = vmm
                vmm += 1; mm[("t2", sc, pr)] = vmm
                vmm += 1; mm[("up", sc, pr)] = vmm
                for j in range(G // 2):
                    vact += 1; act[("corr", sc, pr, j)] = vact
                vact += 2; act[("tmp", sc, pr)] = vact
                vdve += 4; dve[("qg", sc, pr)] = vdve
                vdve += 2; dve[("stm", sc, pr)] = vdve
                vdve += 1; dve[("z1i", sc, pr)] = vdve
                for j in range(G // 2):
                    vdve += 1; dve[("mv", sc, pr, j)] = vdve
                vdve += 2; dve[("z1co", sc, pr)] = vdve
                vdve += 1; dve[("zb", sc, pr)] = vdve
                vdve += 3; dve[("wu", sc, pr)] = vdve
            mm_after_sc[sc] = vmm
            dve_after_sc[sc] = vdve

        @blk.gpsimd
        def _(g):
            s_w1.inc(g.dma_start(
                out=w1f[:, :].rearrange("p (n c) -> p n c", n=NPAIR),
                in_=io["w1bd"][:, :, :].rearrange("n p c -> p n c")), 16)
            for sc in range(NSC):
                buf = sc % 2
                g.wait_ge(s_dve.h, dve[("zb", sc, NPAIR - 1)])
                s_stz.inc(g.dma_start(out=sZ[seq, sc * 128:(sc + 1) * 128, :],
                                      in_=zout[:, buf * D:(buf + 1) * D]), 16)
            g.wait_ge(s_stz.h, s_stz.v)

        @blk.sync
        def _(sy):
            for sc in range(NSC):
                buf = sc % 2
                if sc >= 2:
                    sy.wait_ge(s_mm.h, mm_after_sc[sc - 2])
                    sy.wait_ge(s_dve.h, dve_after_sc[sc - 2])
                s_ldb.inc(sy.dma_start(
                    out=xbf[:, buf * CW:(buf + 1) * CW].rearrange("p (g t) -> p g t", g=NCHUNK),
                    in_=sXB[seq].rearrange("(g p) t -> p g t", p=128)[:, :, sc * 128:(sc + 1) * 128]), 16)
                s_ldb.inc(sy.dma_start(
                    out=xcf[:, buf * CW:(buf + 1) * CW].rearrange("p (g t) -> p g t", g=NCHUNK),
                    in_=sXC[seq].rearrange("(g p) t -> p g t", p=128)[:, :, sc * 128:(sc + 1) * 128]), 16)
                s_ldb.inc(sy.dma_start(
                    out=xbt[:, buf * D:(buf + 1) * D],
                    in_=sXB[seq][:, sc * 128:(sc + 1) * 128], transpose=True), 16)
                s_ldb.inc(sy.dma_start(
                    out=xa[:, buf * D:(buf + 1) * D],
                    in_=sXA[seq][sc * 128:(sc + 1) * 128, :]), 16)

        @blk.tensor
        def _(t):
            t.wait_ge(s_dve.h, dve_w1copy)
            for sc in range(NSC):
                buf = sc % 2
                tb = sc
                t.wait_ge(s_ldb.h, 64 * (sc + 1))
                for pr in range(NPAIR):
                    pc = buf * CW + pr * 128
                    w1c = pr * 128
                    if sc > 0 or pr > 0:
                        psc, ppr = (sc, pr - 1) if pr > 0 else (sc - 1, NPAIR - 1)
                        t.wait_ge(s_dve.h, dve[("wu", psc, ppr)])
                        t.wait_ge(s_act.h, act[("tmp", psc, ppr)])
                    t.matmul(b_sA[:, :], xbf[0:64, pc:pc + 128], xbf[0:64, pc:pc + 128],
                             start=True, stop=True)
                    t.matmul(b_sB[:, :], xbf[64:128, pc:pc + 128], xbf[64:128, pc:pc + 128],
                             start=True, stop=True)
                    t.matmul(b_gA[:, :], xbf[0:64, pc:pc + 128], xcf[0:64, pc:pc + 128],
                             start=True, stop=True)
                    t.matmul(b_gB[:, :], xbf[64:128, pc:pc + 128], xcf[64:128, pc:pc + 128],
                             start=True, stop=True).then_inc(s_mm.h, 1)
                    t.matmul(b_zb[:, :], xbf[:, pc:pc + 128], w1b[:, w1c:w1c + 128],
                             start=True, stop=True).then_inc(s_mm.h, 1)
                    for j in range(G // 2):
                        t.wait_ge(s_dve.h, dve[("mv", sc, pr, j - 1)] if j > 0
                                  else dve[("z1i", sc, pr)])
                        K = 32 * j + 16
                        r0, r1 = 32 * j, 32 * j + 32
                        t.matmul(b_v[r0:r1, 0:64], stA[0:K, r0:r1],
                                 z1[0:K, 0:64], start=True, stop=True,
                                 tile_position=(0, r0))
                        t.matmul(b_v[r0:r1, 64:128], stB[0:K, r0:r1],
                                 z1[0:K, 64:128], start=False, stop=True,
                                 tile_position=(0, r0)).then_inc(s_mm.h, 1)
                        t.wait_ge(s_act.h, act[("corr", sc, pr, j)])
                        t.matmul(b_v[r0:r1, 0:64], stA[r0:r0 + 16, r0:r1],
                                 corr[r0:r0 + 16, 0:64], start=False, stop=True,
                                 tile_position=(r0, r0))
                        t.matmul(b_v[r0:r1, 64:128], stB[r0:r0 + 16, r0:r1],
                                 corr[r0:r0 + 16, 64:128], start=False, stop=True,
                                 tile_position=(r0, r0)).then_inc(s_mm.h, 1)
                    t.wait_ge(s_dve.h, dve[("mv", sc, pr, G // 2 - 1)])
                    t.matmul(b_zb[:, :], xcf[:, pc:pc + 128], w1b[:, w1c:w1c + 128],
                             start=True, stop=True)
                    t.matmul(b_zb[:, 0:64], qoffA[:, :], z1[:, 0:64],
                             start=False, stop=True)
                    t.matmul(b_zb[:, 64:128], qoffB[:, :], z1[:, 64:128],
                             start=False, stop=True)
                    t.matmul(b_d[:, 0:64], gdnA[:, :], z1[:, 0:64],
                             start=True, stop=True)
                    t.matmul(b_d[:, 64:128], gdnB[:, :], z1[:, 64:128],
                             start=False, stop=True).then_inc(s_mm.h, 1)
                    t.wait_ge(s_dve.h, dve[("z1co", sc, pr)])
                    t.matmul(b_w[:, :], xbt[:, buf * D + pr * 128: buf * D + (pr + 1) * 128],
                             z1co[:, :], start=True, stop=True).then_inc(s_mm.h, 1)

        @blk.vector
        def _(v):
            v.wait_ge(s_w1.h, 16)
            s_dve.inc(v.tensor_copy(w1b[:, :], w1f[:, :]), 1)
            for sc in range(NSC):
                buf = sc % 2
                tb = sc
                for pr in range(NPAIR):
                    colA = (seq * NBLK + tb) * NH + 2 * pr
                    coA = coeff_sb[:, colA:colA + 1]
                    coB = coeff_sb[:, colA + 1:colA + 2]
                    v.wait_ge(s_mm.h, mm[("sg", sc, pr)])
                    s_dve.inc(v.scalar_tensor_tensor(qoffA[:, :], b_gA[:, :], coA,
                                                     mko[:, :], ALU.mult, ALU.mult), 1)
                    s_dve.inc(v.scalar_tensor_tensor(qoffB[:, :], b_gB[:, :], coB,
                                                     mko[:, :], ALU.mult, ALU.mult), 1)
                    s_dve.inc(v.tensor_mul(gdnA[:, :], b_gA[:, :], mkd[:, :]), 1)
                    s_dve.inc(v.tensor_mul(gdnB[:, :], b_gB[:, :], mkd[:, :]), 1)
                    s_dve.inc(v.scalar_tensor_tensor(stA[:, :], b_sA[:, :], coA,
                                                     msv[:, :], ALU.mult, ALU.mult), 1)
                    s_dve.inc(v.scalar_tensor_tensor(stB[:, :], b_sB[:, :], coB,
                                                     msv[:, :], ALU.mult, ALU.mult), 1)
                    v.wait_ge(s_mm.h, mm[("r", sc, pr)])
                    s_dve.inc(v.tensor_sub(z1[:, :], b_zb[:, :],
                                           xa[:, buf * D + pr * 128: buf * D + (pr + 1) * 128]), 1)
                    for j in range(G // 2):
                        v.wait_ge(s_mm.h, mm[("svc", sc, pr, j)])
                        s_dve.inc(v.tensor_sub(z1[32 * j:32 * (j + 1), :],
                                               z1[32 * j:32 * (j + 1), :],
                                               b_v[32 * j:32 * (j + 1), :]), 1)
                    s_dve.inc(v.tensor_scalar_mul(z1co[:, 0:64], z1[:, 0:64], coA), 1)
                    s_dve.inc(v.tensor_scalar_mul(z1co[:, 64:128], z1[:, 64:128], coB), 1)
                    if sc >= 2 and pr == 0:
                        v.wait_ge(s_stz.h, 16 * (sc - 1))
                    v.wait_ge(s_mm.h, mm[("t2", sc, pr)])
                    v.wait_ge(s_act.h, act[("tmp", sc, pr)])
                    s_dve.inc(v.tensor_add(zout[:, buf * D + pr * 128: buf * D + (pr + 1) * 128],
                                           b_zb[:, :], tmp[:, :]), 1)
                    v.wait_ge(s_mm.h, mm[("up", sc, pr)])
                    wc = pr * 128
                    s_dve.inc(v.tensor_sub(w1f[0:64, wc:wc + 64], w1f[0:64, wc:wc + 64],
                                           b_w[0:64, 0:64]), 1)
                    s_dve.inc(v.tensor_sub(w1f[64:128, wc + 64:wc + 128],
                                           w1f[64:128, wc + 64:wc + 128],
                                           b_w[64:128, 64:128]), 1)
                    s_dve.inc(v.tensor_copy(w1b[:, wc:wc + 128], w1f[:, wc:wc + 128]), 1)

        @blk.scalar
        def _(sa):
            for sc in range(NSC):
                tb = sc
                for pr in range(NPAIR):
                    colA = (seq * NBLK + tb) * NH + 2 * pr
                    coA = coeff_sb[:, colA:colA + 1]
                    coB = coeff_sb[:, colA + 1:colA + 2]
                    for j in range(G // 2):
                        sa.wait_ge(s_mm.h, mm[("sva", sc, pr, j)])
                        s_act.inc(sa.activation(corr[32 * j:32 * j + 16, :],
                                                b_v[32 * j:32 * j + 16, :],
                                                AF.Identity, scale=-1.0), 1)
                    sa.wait_ge(s_mm.h, mm[("t2", sc, pr)])
                    s_act.inc(sa.activation(tmp[:, 0:64], b_d[:, 0:64], AF.Identity, scale=coA), 1)
                    s_act.inc(sa.activation(tmp[:, 64:128], b_d[:, 64:128], AF.Identity, scale=coB), 1)

    _es.close()

def _emit_C(nc, seq, io, ps):
    sZ, sH, x_tm, woT = io["sZ"], io["sH"], io["x"], io["woT"]
    eps_sb = ps["eps"]
    NT = 4
    BPT = 4
    ZTW = NCHUNK * 512

    from contextlib import ExitStack
    _es = ExitStack()
    wo_sb = _es.enter_context(nc.sbuf_tensor(f"c_wo{seq}", [128, NCHUNK * D], BF16))
    zt_sb = _es.enter_context(nc.sbuf_tensor(f"c_zt{seq}", [128, 2 * ZTW], BF16))
    x_sb = _es.enter_context(nc.sbuf_tensor(f"c_x{seq}", [128, 2 * D], F32))
    u_sb = _es.enter_context(nc.sbuf_tensor(f"c_u{seq}", [128, 2 * D], F32))
    h_sb = _es.enter_context(nc.sbuf_tensor(f"c_h{seq}", [128, 2 * D], BF16))
    sq_sb = _es.enter_context(nc.sbuf_tensor(f"c_sq{seq}", [128, D], BF16))
    ln_sb = _es.enter_context(nc.sbuf_tensor(f"c_ln{seq}", [128, 8], F32))
    pc0 = _es.enter_context(nc.psum_tensor(f"pc0{seq}", [128, 512], F32))
    pc1 = _es.enter_context(nc.psum_tensor(f"pc1{seq}", [128, 512], F32))
    pc2 = _es.enter_context(nc.psum_tensor(f"pc2{seq}", [128, 512], F32))
    pc3 = _es.enter_context(nc.psum_tensor(f"pc3{seq}", [128, 512], F32))
    blk = _es.enter_context(nc.Block())
    banks = [pc0, pc1, pc2, pc3]

    s_zt = Ctr(nc, f"c_szt{seq}")
    s_gp = Ctr(nc, f"c_sgp{seq}")
    s_mm = Ctr(nc, f"c_smm{seq}")
    s_dve = Ctr(nc, f"c_sdve{seq}")
    s_act = Ctr(nc, f"c_sact{seq}")

    gp = {}
    v = 16 * NCHUNK
    for bb in range(NBLK):
        v += 16; gp[("x", bb)] = v
        if bb >= 2:
            v += 16; gp[("sh", bb - 2)] = v
    for bb in (NBLK - 2, NBLK - 1):
        v += 16; gp[("sh", bb)] = v
    mm = {}
    v = 0
    for bb in range(NBLK):
        for ns in range(NS):
            v += 1; mm[("b", bb, ns)] = v
    dve = {}
    v = 0
    for bb in range(NBLK):
        for ns in range(NS):
            v += 1; dve[("ev", bb, ns)] = v
        v += 1; dve[("sum", bb)] = v
        v += 3; dve[("var", bb)] = v
        v += 2; dve[("nm", bb)] = v
    act = {}
    v = 0
    for bb in range(NBLK):
        v += 1; act[("sq", bb)] = v
        v += 1; act[("sd", bb)] = v
        v += 1; act[("h", bb)] = v

    @blk.gpsimd
    def _(g):
        for c in range(NCHUNK):
            s_gp.inc(g.dma_start(out=wo_sb[:, c * D:(c + 1) * D],
                                 in_=woT[128 * c:128 * (c + 1), :]), 16)
        for bb in range(NBLK):
            xb = bb % 2
            if bb >= 2:
                g.wait_ge(s_dve.h, dve[("ev", bb - 2, NS - 1)])
            s_gp.inc(g.dma_start(out=x_sb[:, xb * D:(xb + 1) * D],
                                 in_=x_tm[seq, 128 * bb:128 * (bb + 1), :]), 16)
            if bb >= 2:
                g.wait_ge(s_act.h, act[("h", bb - 2)])
                s_gp.inc(g.dma_start(out=sH[seq, 128 * (bb - 2):128 * (bb - 1), :],
                                     in_=h_sb[:, (bb % 2) * D:(bb % 2 + 1) * D]), 16)
        for bb in (NBLK - 2, NBLK - 1):
            g.wait_ge(s_act.h, act[("h", bb)])
            s_gp.inc(g.dma_start(out=sH[seq, 128 * bb:128 * (bb + 1), :],
                                 in_=h_sb[:, (bb % 2) * D:(bb % 2 + 1) * D]), 16)
        g.wait_ge(s_gp.h, s_gp.v)

    @blk.sync
    def _(sy):
        for tile in range(NT):
            tb = tile % 2
            if tile >= 2:
                sy.wait_ge(s_mm.h, mm[("b", (tile - 1) * BPT - 1, NS - 1)])
            for dz in range(NCHUNK):
                s_zt.inc(sy.dma_start(
                    out=zt_sb[:, tb * ZTW + dz * 512:tb * ZTW + (dz + 1) * 512],
                    in_=sZ[seq][tile * 512:(tile + 1) * 512, 128 * dz:128 * (dz + 1)],
                    transpose=True), 16)

    @blk.tensor
    def _(t):
        t.wait_ge(s_gp.h, 16 * NCHUNK)
        for bb in range(NBLK):
            tile, sub = bb // BPT, bb % BPT
            tb = tile % 2
            t.wait_ge(s_zt.h, 16 * NCHUNK * (tile + 1))
            if bb >= 1:
                t.wait_ge(s_dve.h, dve[("ev", bb - 1, NS - 1)])
            for ns in range(NS):
                last = None
                for dz in range(NCHUNK):
                    base = tb * ZTW + dz * 512 + sub * 128
                    last = t.matmul(
                        banks[ns][:, :], zt_sb[:, base:base + 128],
                        wo_sb[:, dz * D + ns * 512: dz * D + (ns + 1) * 512],
                        start=(dz == 0), stop=(dz == NCHUNK - 1))
                last.then_inc(s_mm.h, 1)

    @blk.vector
    def _(v):
        for bb in range(NBLK):
            xb = bb % 2
            v.wait_ge(s_gp.h, gp[("x", bb)])
            if bb >= 2:
                v.wait_ge(s_act.h, act[("h", bb - 2)])
            for ns in range(NS):
                v.wait_ge(s_mm.h, mm[("b", bb, ns)])
                s_dve.inc(v.tensor_add(u_sb[:, xb * D + ns * 512: xb * D + (ns + 1) * 512],
                                       banks[ns][:, :],
                                       x_sb[:, xb * D + ns * 512: xb * D + (ns + 1) * 512]), 1)
            s_dve.inc(v.reduce_sum(ln_sb[:, 0:1], u_sb[:, xb * D:(xb + 1) * D], axis=AX.X), 1)
            v.wait_ge(s_dve.h, s_dve.v)
            v.wait_ge(s_act.h, act[("sq", bb)])
            s_dve.inc(v.tensor_scalar_mul(ln_sb[:, 2:3], ln_sb[:, 0:1], 1.0 / D), 1)
            v.wait_ge(s_dve.h, s_dve.v)
            s_dve.inc(v.tensor_scalar(ln_sb[:, 3:4], ln_sb[:, 2:3], ln_sb[:, 2:3], None,
                                      ALU.mult), 1)
            v.wait_ge(s_dve.h, s_dve.v)
            s_dve.inc(v.scalar_tensor_tensor(ln_sb[:, 4:5], ln_sb[:, 1:2], 1.0 / D,
                                             ln_sb[:, 3:4], ALU.mult, ALU.subtract), 1)
            v.wait_ge(s_act.h, act[("sd", bb)])
            s_dve.inc(v.reciprocal(ln_sb[:, 6:7], ln_sb[:, 5:6]), 1)
            v.wait_ge(s_dve.h, s_dve.v)
            s_dve.inc(v.tensor_scalar(ln_sb[:, 7:8], ln_sb[:, 2:3], ln_sb[:, 6:7], -1.0,
                                      ALU.mult, ALU.mult), 1)

    @blk.scalar
    def _(sa):
        for bb in range(NBLK):
            xb = bb % 2
            sa.wait_ge(s_dve.h, dve[("sum", bb)])
            sa.activation(sq_sb[:, :], u_sb[:, xb * D:(xb + 1) * D],
                          AF.Square, accum_out=ln_sb[:, 1:2])
            s_act.inc(sa.copy(ln_sb[:, 1:2], ln_sb[:, 1:2]), 1)
            sa.wait_ge(s_dve.h, dve[("var", bb)])
            s_act.inc(sa.activation(ln_sb[:, 5:6], ln_sb[:, 4:5], AF.Sqrt,
                                    bias=eps_sb[:, 0:1]), 1)
            sa.wait_ge(s_dve.h, dve[("nm", bb)])
            if bb >= 2:
                sa.wait_ge(s_gp.h, gp[("sh", bb - 2)])
            s_act.inc(sa.activation(h_sb[:, xb * D:(xb + 1) * D],
                                    u_sb[:, xb * D:(xb + 1) * D], AF.Identity,
                                    scale=ln_sb[:, 6:7], bias=ln_sb[:, 7:8]), 1)
    _es.close()


def _emit_Dinit(nc, io, ps):
    from contextlib import ExitStack
    _es = ExitStack()
    wf, wfT, fbias = io["wf"], io["wfT"], io["fbias_row"]
    wq_sb = ps["wq"]; wsb_sb = ps["wsb"]; bias_sb = ps["biasb"]
    wt_sb = _es.enter_context(nc.sbuf_tensor("d_wt", [128, 2 * D], F32))
    wsc_sb = _es.enter_context(nc.sbuf_tensor("d_wsc", [128, 32], F32))
    wsr_sb = _es.enter_context(nc.sbuf_tensor("d_wsr", [32, 128], F32))
    brow_sb = _es.enter_context(nc.sbuf_tensor("d_brow", [1, D], F32))
    t_sb = _es.enter_context(nc.sbuf_tensor("d_t", [128, D], mybir.dt.int32))
    sel_sb = _es.enter_context(nc.sbuf_tensor("d_sel", [32, 2 * D], F32))
    onesf_sb = _es.enter_context(nc.sbuf_tensor("d_onesf", [1, 128], F32))
    identf_sb = _es.enter_context(nc.sbuf_tensor("d_identf", [128, 128], F32))
    p_tr = _es.enter_context(nc.psum_tensor("pd_tr", [128, 512], F32))
    p_bc = _es.enter_context(nc.psum_tensor("pd_bc", [128, 512], F32))
    blk = _es.enter_context(nc.Block())

    s_ld = Ctr(nc, "d_ld")
    s_mm = Ctr(nc, "d_mm")
    s_act = Ctr(nc, "d_act")
    s_dve = Ctr(nc, "d_dve")
    ld = {}
    dve = {}
    mm = {}

    v = 0
    for ob in range(NCHUNK):
        v += 1; dve[("amax", ob)] = v
    v += 4; dve["scales"] = v
    v += 1; dve["wsr"] = v
    for ob in range(NCHUNK):
        v += 1; dve[("wsev", ob)] = v
    for ob in range(NCHUNK):
        v += 1; dve[("rwsev", ob)] = v
    for sl in range(NS):
        v += 4; dve[("bev", sl)] = v
    for dz in range(NCHUNK):
        v += 1; dve[("wq", dz)] = v
    v = 0
    v += 1; mm["tr"] = v
    for ob in range(NCHUNK):
        v += 1; mm[("ws", ob)] = v
    for ob in range(NCHUNK):
        v += 1; mm[("rws", ob)] = v
    for sl in range(NS):
        v += 1; mm[("bias", sl)] = v

    @blk.gpsimd
    def _(g):
        s_ld.inc(g.dma_start(out=sel_sb[:, :], in_=io["selbc"][:, :]), 16)
        s_ld.inc(g.dma_start(out=onesf_sb[:, :], in_=io["onesf32"][:, :]), 16)
        s_ld.inc(g.dma_start(out=identf_sb[:, :], in_=io["identf32"][:, :]), 16)
        s_ld.inc(g.dma_start(out=brow_sb[:, :], in_=fbias[:, :]), 16)
        for ob in range(NCHUNK):
            if ob >= 2:
                g.wait_ge(s_dve.h, dve[("amax", ob - 2)])
            s_ld.inc(g.dma_start(out=wt_sb[:, (ob % 2) * D:(ob % 2 + 1) * D],
                                 in_=wf[128 * ob:128 * (ob + 1), :]), 16)
            ld[("wf", ob)] = s_ld.v
        for dz in range(NCHUNK):
            if dz < 2:
                g.wait_ge(s_dve.h, dve[("amax", NCHUNK - 1)])
            else:
                g.wait_ge(s_dve.h, dve[("wq", dz - 2)])
            s_ld.inc(g.dma_start(out=wt_sb[:, (dz % 2) * D:(dz % 2 + 1) * D],
                                 in_=wfT[128 * dz:128 * (dz + 1), :]), 16)
            ld[("wfT", dz)] = s_ld.v

    @blk.tensor
    def _(t):
        t.wait_ge(s_ld.h, 64)
        t.wait_ge(s_dve.h, dve["scales"])
        t.transpose(p_tr[0:32, 0:128], wsc_sb[:, 0:32], identf_sb[:, :]).then_inc(s_mm.h, 1)
        t.wait_ge(s_dve.h, dve["wsr"])
        for ob in range(NCHUNK):
            if ob >= 1:
                t.wait_ge(s_dve.h, dve[("wsev", ob - 1)])
            t.matmul(p_bc[:, 0:128], sel_sb[:, ob * 128:(ob + 1) * 128],
                     wsr_sb[0:32, :], start=True, stop=True).then_inc(s_mm.h, 1)
        for ob in range(NCHUNK):
            if ob >= 1:
                t.wait_ge(s_dve.h, dve[("rwsev", ob - 1)])
            else:
                t.wait_ge(s_dve.h, dve[("wsev", NCHUNK - 1)])
            t.matmul(p_bc[:, 128:256], sel_sb[:, D + ob * 128:D + (ob + 1) * 128],
                     wsr_sb[0:32, :], start=True, stop=True).then_inc(s_mm.h, 1)
        for sl in range(NS):
            if sl >= 1:
                t.wait_ge(s_dve.h, dve[("bev", sl - 1)])
            else:
                t.wait_ge(s_dve.h, dve[("rwsev", NCHUNK - 1)])
            t.matmul(p_bc[:, 256:384], onesf_sb[0:1, :],
                     brow_sb[0:1, sl * 512:sl * 512 + 128], start=True, stop=True)
            t.matmul(p_bc[:, 384:512], onesf_sb[0:1, :],
                     brow_sb[0:1, sl * 512 + 128:sl * 512 + 256], start=True, stop=True)
            t.matmul(p_tr[:, 256:384], onesf_sb[0:1, :],
                     brow_sb[0:1, sl * 512 + 256:sl * 512 + 384], start=True, stop=True)
            t.matmul(p_tr[:, 384:512], onesf_sb[0:1, :],
                     brow_sb[0:1, sl * 512 + 384:sl * 512 + 512], start=True,
                     stop=True).then_inc(s_mm.h, 1)

    @blk.vector
    def _(v):
        for ob in range(NCHUNK):
            v.wait_ge(s_ld.h, ld[("wf", ob)])
            s_dve.inc(v.reduce_max(wsc_sb[:, ob:ob + 1],
                                   wt_sb[:, (ob % 2) * D:(ob % 2 + 1) * D],
                                   axis=AX.X, apply_absolute_value=True), 1)
        v.wait_ge(s_dve.h, s_dve.v)
        s_dve.inc(v.tensor_scalar_max(wsc_sb[:, 0:16], wsc_sb[:, 0:16], 1e-8), 1)
        v.wait_ge(s_dve.h, s_dve.v)
        s_dve.inc(v.reciprocal(wsc_sb[:, 16:32], wsc_sb[:, 0:16]), 1)
        v.wait_ge(s_dve.h, s_dve.v)
        s_dve.inc(v.tensor_scalar_mul(wsc_sb[:, 16:32], wsc_sb[:, 16:32], 127.0), 1)
        v.wait_ge(s_dve.h, s_dve.v)
        s_dve.inc(v.reciprocal(wsc_sb[:, 0:16], wsc_sb[:, 16:32]), 1)
        v.wait_ge(s_dve.h, s_dve.v)

        v.wait_ge(s_mm.h, mm["tr"])
        s_dve.inc(v.tensor_copy(wsr_sb[0:32, :], p_tr[0:32, 0:128]), 1)
        for ob in range(NCHUNK):
            v.wait_ge(s_mm.h, mm[("ws", ob)])
            s_dve.inc(v.tensor_copy(wsb_sb[:, ob * 128:(ob + 1) * 128], p_bc[:, 0:128]), 1)
        for ob in range(NCHUNK):
            v.wait_ge(s_mm.h, mm[("rws", ob)])
            s_dve.inc(v.tensor_copy(wsb_sb[:, D + ob * 128:D + (ob + 1) * 128],
                                    p_bc[:, 128:256]), 1)
        for sl in range(NS):
            v.wait_ge(s_mm.h, mm[("bias", sl)])
            v.tensor_copy(bias_sb[:, sl * 512:sl * 512 + 128], p_bc[:, 256:384])
            v.tensor_copy(bias_sb[:, sl * 512 + 128:sl * 512 + 256], p_bc[:, 384:512])
            v.tensor_copy(bias_sb[:, sl * 512 + 256:sl * 512 + 384], p_tr[:, 256:384])
            s_dve.inc(v.tensor_copy(bias_sb[:, sl * 512 + 384:sl * 512 + 512],
                                    p_tr[:, 384:512]), 4)
        for dz in range(NCHUNK):
            v.wait_ge(s_ld.h, ld[("wfT", dz)])
            if dz >= 1:
                v.wait_ge(s_act.h, dz)
            s_dve.inc(v.tensor_mul(t_sb[:, :], wt_sb[:, (dz % 2) * D:(dz % 2 + 1) * D],
                                   wsb_sb[:, D:2 * D]), 1)

    @blk.scalar
    def _(sa):
        for dz in range(NCHUNK):
            sa.wait_ge(s_dve.h, dve[("wq", dz)])
            s_act.inc(sa.activation(wq_sb[:, dz * D:(dz + 1) * D], t_sb[:, :],
                                    AF.Copy), 1)
    _es.close()


def _emit_D(nc, seq, io, ps, mm_part=True):
    import os as _os
    LVL = int(_os.environ.get("D_LEVEL", "4")) if mm_part else 1
    """Phase D: quantize h, FFN matmul, dequant + bias + residual, LN2 -> out.

    Engine-order design (deadlock-free):
      DVE:    q0..q3, q4..q7, [mm0..mm3, q8..q11], [mm4..mm7, q12..q15],
              mm8..mm11, mm12..mm15
      ACT:    hq0..hq7, [ln0..3, hq8..11], [ln4..7, hq12..15], ln8..15
      gpsimd: quant stream (h loads + hq stores) then out stores
      sync:   h2 loads + hqt transposed loads
    """
    sH, sHq, out_t = io["sH"], io["sHq"], io["out"]
    eps_sb = ps["eps"]
    wq_sb = ps["wq"]; wsb_sb = ps["wsb"]; bias_sb = ps["biasb"]
    NT = 4
    BPT = 4
    HTW = NCHUNK * 512

    from contextlib import ExitStack
    _es = ExitStack()
    h_sb = _es.enter_context(nc.sbuf_tensor(f"d_h{seq}", [128, 2 * D], BF16))
    h2_sb = _es.enter_context(nc.sbuf_tensor(f"d_h2{seq}", [128, 2 * D], BF16))
    hb2_sb = _es.enter_context(nc.sbuf_tensor(f"d_hb2{seq}", [128, 2 * D], BF16))
    hq_sb = _es.enter_context(nc.sbuf_tensor(f"d_hq{seq}", [128, 2 * D], BF16))
    hqt_sb = _es.enter_context(nc.sbuf_tensor(f"d_hqt{seq}", [128, HTW], BF16))
    t_sb = _es.enter_context(nc.sbuf_tensor(f"d_t2{seq}", [128, 2 * D], mybir.dt.int32))
    v_sb = _es.enter_context(nc.sbuf_tensor(f"d_v{seq}", [128, 2 * D], F32))
    o_sb = _es.enter_context(nc.sbuf_tensor(f"d_o{seq}", [128, 2 * D], F32))
    sq_sb = _es.enter_context(nc.sbuf_tensor(f"d_sq{seq}", [128, D], BF16))
    ln_sb = _es.enter_context(nc.sbuf_tensor(f"d_ln{seq}", [128, 8], F32))
    ln2_sb = _es.enter_context(nc.sbuf_tensor(f"d_ln2{seq}", [128, 8], F32))
    xs_sb = _es.enter_context(nc.sbuf_tensor(f"d_xs{seq}", [128, NBLK], F32))
    pd0 = _es.enter_context(nc.psum_tensor(f"pd0{seq}", [128, 512], F32))
    pd1 = _es.enter_context(nc.psum_tensor(f"pd1{seq}", [128, 512], F32))
    pd2 = _es.enter_context(nc.psum_tensor(f"pd2{seq}", [128, 512], F32))
    pd3 = _es.enter_context(nc.psum_tensor(f"pd3{seq}", [128, 512], F32))
    blk = _es.enter_context(nc.Block())
    banks = [pd0, pd1, pd2, pd3]

    s_gp = Ctr(nc, f"d_sgp{seq}")
    s_sy = Ctr(nc, f"d_ssy{seq}")
    s_mm = Ctr(nc, f"d_smm{seq}")
    s_dve = Ctr(nc, f"d_sdve{seq}")
    s_act = Ctr(nc, f"d_sact{seq}")
    s_so = Ctr(nc, f"d_sso{seq}")

    def dve_order():
        yield ("q", 0); yield ("q", 1); yield ("q", 2); yield ("q", 3)
        yield ("q", 4); yield ("q", 5); yield ("q", 6); yield ("q", 7)
        nq = 8
        for bb in range(NBLK):
            yield ("mm", bb)
            if nq < NBLK:
                yield ("q", nq); nq += 1

    def act_order():
        for bb in range(8):
            yield ("hq", bb)
        nq = 8
        for bb in range(NBLK):
            yield ("ln", bb)
            if nq < NBLK:
                yield ("hq", nq); nq += 1

    dve = {}
    v = 0
    for kind, bb in dve_order():
        if kind == "q":
            v += 6; dve[("xs", bb)] = v
            v += 1; dve[("hq", bb)] = v
        else:
            v += 1; dve[("hb2", bb)] = v
            for ns in range(NS):
                v += 2; dve[("y", bb, ns)] = v
            v += 1; dve[("sum", bb)] = v
            v += 3; dve[("var", bb)] = v
            v += 2; dve[("nm", bb)] = v
    act = {}
    v = 0
    for kind, bb in act_order():
        if kind == "hq":
            v += 1; act[("hq", bb)] = v
        else:
            v += 1; act[("sq", bb)] = v
            v += 1; act[("sd", bb)] = v
            v += 1; act[("o", bb)] = v
    mm = {}
    v = 0
    for bb in range(NBLK):
        for ns in range(NS):
            v += 1; mm[("b", bb, ns)] = v
    gp = {}
    v = 0
    for bb in range(NBLK):
        v += 16; gp[("h", bb)] = v
        if bb >= 2:
            v += 16; gp[("shq", bb - 2)] = v
    for bb in (NBLK - 2, NBLK - 1):
        v += 16; gp[("shq", bb)] = v
    sy_rec = {}
    v = 0
    v += 16; sy_rec[("h2", 0)] = v
    v += 16; sy_rec[("h2", 1)] = v
    for tile in range(NT):
        for dz in range(NCHUNK):
            v += 16
        sy_rec[("hqt", tile)] = v
        for bb in (tile * BPT + 2, tile * BPT + 3):
            b2 = bb
        if tile < NT:
            pass
    v = sy_rec[("hqt", 0)]
    nxt = 2
    sy2 = dict(sy_rec)
    v = 0
    sy_rec = {}
    v += 16; sy_rec[("h2", 0)] = v
    v += 16; sy_rec[("h2", 1)] = v
    nxt = 2
    for tile in range(NT):
        v += 16 * NCHUNK
        sy_rec[("hqt", tile)] = v
        for k in range(4):
            if nxt < NBLK:
                v += 16; sy_rec[("h2", nxt)] = v; nxt += 1

    @blk.gpsimd
    def _(g):
        for bb in range(NBLK):
            hbuf = bb % 2
            if bb >= 2:
                g.wait_ge(s_dve.h, dve[("hq", bb - 2)])
            s_gp.inc(g.dma_start(out=h_sb[:, hbuf * D:(hbuf + 1) * D],
                                 in_=sH[seq, 128 * bb:128 * (bb + 1), :]), 16)
            if bb >= 2:
                g.wait_ge(s_act.h, act[("hq", bb - 2)])
                s_gp.inc(g.dma_start(out=sHq[seq, 128 * (bb - 2):128 * (bb - 1), :],
                                     in_=hq_sb[:, (bb % 2) * D:(bb % 2 + 1) * D]), 16)
        for bb in (NBLK - 2, NBLK - 1):
            g.wait_ge(s_act.h, act[("hq", bb)])
            s_gp.inc(g.dma_start(out=sHq[seq, 128 * bb:128 * (bb + 1), :],
                                 in_=hq_sb[:, (bb % 2) * D:(bb % 2 + 1) * D]), 16)
        g.wait_ge(s_gp.h, s_gp.v)

    @blk.sync
    def _(sy):
        if LVL < 2:
            return
        def h2load(bb):
            if bb >= 2 and LVL >= 3:
                sy.wait_ge(s_dve.h, dve[("y", bb - 2, NS - 1)] + 0)
            s_sy.inc(sy.dma_start(out=h2_sb[:, (bb % 2) * D:(bb % 2 + 1) * D],
                                  in_=sH[seq, 128 * bb:128 * (bb + 1), :]), 16)
        import os as _os
        if _os.environ.get("D_NOH2") != "1":
            h2load(0)
            h2load(1)
        nxt = 2
        for tile in range(NT):
            sy.wait_ge(s_gp.h, gp[("shq", (tile + 1) * BPT - 1)])
            if tile >= 1 and LVL >= 3:
                sy.wait_ge(s_mm.h, mm[("b", tile * BPT - 1, NS - 1)])
            for dz in range(NCHUNK):
                if _os.environ.get("D_NOHQT") == "1":
                    continue
                if _os.environ.get("D_NOTR") == "1":
                    s_sy.inc(sy.dma_start(
                        out=hqt_sb[:, dz * 512:(dz + 1) * 512],
                        in_=sHq[seq][128 * dz:128 * (dz + 1), tile * 512:(tile + 1) * 512]), 16)
                else:
                    s_sy.inc(sy.dma_start(
                        out=hqt_sb[:, dz * 512:(dz + 1) * 512],
                        in_=sHq[seq][tile * 512:(tile + 1) * 512, 128 * dz:128 * (dz + 1)],
                        transpose=True), 16)
            for k in range(4):
                if nxt < NBLK:
                    if _os.environ.get("D_NOH2") != "1":
                        h2load(nxt)
                    nxt += 1

    @blk.tensor
    def _(t):
        if LVL < 3:
            return
        for bb in range(NBLK):
            tile, sub = bb // BPT, bb % BPT
            t.wait_ge(s_sy.h, sy_rec[("hqt", tile)])
            if bb >= 1:
                t.wait_ge(s_dve.h, dve[("y", bb - 1, NS - 1)])
            for ns in range(NS):
                last = None
                for dz in range(NCHUNK):
                    base = dz * 512 + sub * 128
                    last = t.matmul(
                        banks[ns][:, :], hqt_sb[:, base:base + 128],
                        wq_sb[:, dz * D + ns * 512: dz * D + (ns + 1) * 512],
                        start=(dz == 0), stop=(dz == NCHUNK - 1))
                last.then_inc(s_mm.h, 1)

    @blk.vector
    def _(v):
        def quant(bb):
            hbuf = bb % 2
            hsl = h_sb[:, hbuf * D:(hbuf + 1) * D]
            v.wait_ge(s_gp.h, gp[("h", bb)])
            xcol = xs_sb[:, bb:bb + 1]
            base = dve[("xs", bb)] - 6
            v.reduce_max(ln_sb[:, 0:1], hsl, axis=AX.X,
                         apply_absolute_value=True).then_inc(s_dve.h, 1)
            v.wait_ge(s_dve.h, base + 1)
            v.reciprocal(ln_sb[:, 1:2], ln_sb[:, 0:1]).then_inc(s_dve.h, 1)
            v.wait_ge(s_dve.h, base + 2)
            v.tensor_scalar_mul(ln_sb[:, 1:2], ln_sb[:, 1:2], 127.0).then_inc(s_dve.h, 1)
            v.wait_ge(s_dve.h, base + 3)
            v.reciprocal(ln_sb[:, 0:1], ln_sb[:, 1:2]).then_inc(s_dve.h, 1)
            v.wait_ge(s_dve.h, base + 4)
            v.tensor_scalar_max(xcol, ln_sb[:, 0:1], 1e-12).then_inc(s_dve.h, 1)
            v.wait_ge(s_dve.h, base + 5)
            v.reciprocal(ln_sb[:, 1:2], xcol).then_inc(s_dve.h, 1)
            v.wait_ge(s_dve.h, base + 6)
            if bb >= 2:
                v.wait_ge(s_act.h, act[("hq", bb - 2)])
            v.tensor_scalar_mul(t_sb[:, hbuf * D:(hbuf + 1) * D], hsl,
                                ln_sb[:, 1:2]).then_inc(s_dve.h, 1)

        def mmpart(bb):
            vbuf = bb % 2
            xcol = xs_sb[:, bb:bb + 1]
            v.wait_ge(s_sy.h, sy_rec[("h2", bb)])
            if bb >= 2 and LVL >= 4:
                v.wait_ge(s_act.h, act[("o", bb - 2)])
            v.tensor_add(hb2_sb[:, vbuf * D:(vbuf + 1) * D],
                         h2_sb[:, vbuf * D:(vbuf + 1) * D],
                         bias_sb[:, :]).then_inc(s_dve.h, 1)
            for ns in range(NS):
                v.wait_ge(s_mm.h, mm[("b", bb, ns)])
                vsl = v_sb[:, vbuf * D + ns * 512: vbuf * D + (ns + 1) * 512]
                v.scalar_tensor_tensor(vsl, banks[ns][:, :], xcol,
                                       wsb_sb[:, ns * 512:(ns + 1) * 512],
                                       ALU.mult, ALU.mult).then_inc(s_dve.h, 1)
                v.tensor_add(vsl, vsl,
                             hb2_sb[:, vbuf * D + ns * 512: vbuf * D + (ns + 1) * 512]).then_inc(s_dve.h, 1)
            if LVL < 4:
                return
            v.reduce_sum(ln2_sb[:, 2:3], v_sb[:, vbuf * D:(vbuf + 1) * D],
                         axis=AX.X).then_inc(s_dve.h, 1)
            v.wait_ge(s_dve.h, dve[("sum", bb)])
            v.wait_ge(s_act.h, act[("sq", bb)])
            v.tensor_scalar_mul(ln2_sb[:, 4:5], ln2_sb[:, 2:3], 1.0 / D).then_inc(s_dve.h, 1)
            v.wait_ge(s_dve.h, dve[("sum", bb)] + 1)
            v.tensor_scalar(ln2_sb[:, 5:6], ln2_sb[:, 4:5], ln2_sb[:, 4:5], None,
                            ALU.mult).then_inc(s_dve.h, 1)
            v.wait_ge(s_dve.h, dve[("sum", bb)] + 2)
            v.scalar_tensor_tensor(ln2_sb[:, 6:7], ln2_sb[:, 3:4], 1.0 / D,
                                   ln2_sb[:, 5:6], ALU.mult, ALU.subtract).then_inc(s_dve.h, 1)
            v.wait_ge(s_act.h, act[("sd", bb)])
            v.reciprocal(ln2_sb[:, 7:8], ln2_sb[:, 5:6]).then_inc(s_dve.h, 1)
            v.wait_ge(s_dve.h, dve[("nm", bb)] - 1)
            v.tensor_scalar(ln2_sb[:, 3:4], ln2_sb[:, 4:5], ln2_sb[:, 7:8], -1.0,
                            ALU.mult, ALU.mult).then_inc(s_dve.h, 1)

        for kind, bb in dve_order():
            if kind == "q":
                quant(bb)
            elif LVL >= 3:
                mmpart(bb)

    @blk.scalar
    def _(sa):
        def hq(bb):
            hbuf = bb % 2
            sa.wait_ge(s_dve.h, dve[("hq", bb)])
            if bb >= 2:
                sa.wait_ge(s_gp.h, gp[("shq", bb - 2)])
            sa.activation(hq_sb[:, hbuf * D:(hbuf + 1) * D],
                          t_sb[:, hbuf * D:(hbuf + 1) * D],
                          AF.Copy).then_inc(s_act.h, 1)

        def ln(bb):
            vbuf = bb % 2
            sa.wait_ge(s_dve.h, dve[("sum", bb)])
            sa.activation(sq_sb[:, :], v_sb[:, vbuf * D:(vbuf + 1) * D],
                          AF.Square, accum_out=ln2_sb[:, 3:4])
            sa.copy(ln2_sb[:, 3:4], ln2_sb[:, 3:4]).then_inc(s_act.h, 1)
            sa.wait_ge(s_dve.h, dve[("var", bb)])
            sa.activation(ln2_sb[:, 5:6], ln2_sb[:, 6:7], AF.Sqrt,
                          bias=eps_sb[:, 0:1]).then_inc(s_act.h, 1)
            sa.wait_ge(s_dve.h, dve[("nm", bb)])
            if bb >= 2:
                sa.wait_ge(s_so.h, 16 * (bb - 1))
            sa.activation(o_sb[:, vbuf * D:(vbuf + 1) * D],
                          v_sb[:, vbuf * D:(vbuf + 1) * D], AF.Identity,
                          scale=ln2_sb[:, 7:8], bias=ln2_sb[:, 3:4]).then_inc(s_act.h, 1)
            sa.wait_ge(s_act.h, act[("o", bb)])
            s_so.inc(sa.dma_start(out=out_t[seq, 128 * bb:128 * (bb + 1), :],
                                  in_=o_sb[:, vbuf * D:(vbuf + 1) * D]), 16)

        for kind, bb in act_order():
            if kind == "hq":
                hq(bb)
            elif LVL >= 4:
                ln(bb)
        if LVL >= 4:
            sa.wait_ge(s_so.h, s_so.v)
    _es.close()



def _selbc():
    sel = np.zeros((32, 2 * D), np.float32)
    for ob in range(NCHUNK):
        sel[ob, ob * 128:(ob + 1) * 128] = 1.0
        sel[16 + ob, D + ob * 128:D + (ob + 1) * 128] = 1.0
    return sel


def _host_inputs(inputs):
    x = np.asarray(inputs["L0_attn_hidden_states"], np.float32)
    wq = np.asarray(inputs["L0_attn_Wq"], np.float32)
    wk = np.asarray(inputs["L0_attn_Wk"], np.float32)
    wv = np.asarray(inputs["L0_attn_Wv"], np.float32)
    wo = np.asarray(inputs["L0_attn_Wo"], np.float32)
    wilr = np.asarray(inputs["L0_attn_W_ilr"], np.float32)
    bilr = np.asarray(inputs["L0_attn_b_ilr"], np.float32)
    w1 = np.asarray(inputs["L0_attn_W1"], np.float32)
    wfw = np.asarray(inputs["L0_ffn_weight"], np.float32)
    fbias = np.asarray(inputs["L0_ffn_bias"], np.float32)

    w1bd = np.zeros((NPAIR, 128, 128), np.float32)
    for p in range(NPAIR):
        w1bd[p, 0:64, 0:64] = w1[2 * p]
        w1bd[p, 64:128, 64:128] = w1[2 * p + 1]

    idx = np.arange(128)
    chs = idx[:, None] // CS
    cht = idx[None, :] // CS
    maskoff_neg = np.where(chs < cht, -1.0, 0.0).astype(np.float32)
    maskdiag_neg = np.where((chs == cht) & (idx[:, None] <= idx[None, :]),
                            -1.0, 0.0).astype(np.float32)
    tokidx = (((idx % CS) + 1).astype(np.float32)).reshape(128, 1)

    masksv = np.where(chs < cht, 1.0, 0.0).astype(np.float32)
    shared = dict(
        masksv=masksv,
        wqT=np.ascontiguousarray(wq.T), wkT=np.ascontiguousarray(wk.T),
        wvT=np.ascontiguousarray(wv.T), woT=np.ascontiguousarray(wo.T),
        wilrT=np.ascontiguousarray(wilr.T),
        bilr_bc=np.broadcast_to(bilr, (128, NH)).copy(),
        w1bd=w1bd, wf=np.ascontiguousarray(wfw),
        wfT=np.ascontiguousarray(wfw.T),
        fbias_row=fbias.reshape(1, D).copy(),
        maskoff_neg=maskoff_neg, maskdiag_neg=maskdiag_neg, tokidx=tokidx,
        epsc=np.full((128, 1), LN_EPS, np.float32),
        onesbf=np.ones((1, 128), ml_dtypes.bfloat16),
        identbf=np.eye(128, dtype=ml_dtypes.bfloat16),
        onesf32=np.ones((1, 128), np.float32),
        identf32=np.eye(128, dtype=np.float32),
        selbc=_selbc(),
    )

    in_maps = []
    for core in range(8):
        xs = x[core * NSEQ:(core + 1) * NSEQ]
        m = dict(shared)
        m["x"] = np.ascontiguousarray(xs)
        m["xT"] = np.ascontiguousarray(xs.transpose(0, 2, 1))
        in_maps.append(m)
    return in_maps


def kernel(**inputs):
    in_maps = _host_inputs(inputs)
    nc = build_program()
    res = run_bass_kernel_spmd(nc, in_maps, core_ids=list(range(8)))
    outs = [r["out"] for r in res.results]
    full = np.concatenate(outs, axis=0).reshape(16, L, D).astype(np.float32)
    if DEBUG_DUMPS:
        kernel.last_results = res.results
    return full
